# version 11
# baseline (speedup 1.0000x reference)
import os
import math
import numpy as np
import ml_dtypes
import concourse.bass as bass
import concourse.mybir as mybir
from concourse.bass_utils import run_bass_kernel_spmd

F32 = mybir.dt.float32
F16 = mybir.dt.float16
BF16 = mybir.dt.bfloat16
FP8 = mybir.dt.float8e4
I32 = mybir.dt.int32
AF = mybir.ActivationFunctionType
OP = mybir.AluOpType

SIMQ = os.environ.get("BASS_SIM_FP64") == "1"

B, S, D, H = 8, 4096, 1024, 4096
NT = 512
NBLK = S // NT
KD = D // 128
KH = H // 128
LN2C = 0.6931471805599453

ENG = ("t", "v", "s", "g", "y")


def build_program():
    nc = bass.Bass("TRN2", target_bir_lowering=False, debug=False, detect_race_conditions=False)

    xT = nc.dram_tensor("xT", [D, S], F32, kind="ExternalInput").ap()
    Wat = {}
    for nm in ("W_in", "W_gx", "W_ga", "W_out"):
        Wat[nm] = nc.dram_tensor(nm, [D, D], F32, kind="ExternalInput").ap()
    w1T = nc.dram_tensor("w1T", [D, H], F32, kind="ExternalInput").ap()
    w2T = nc.dram_tensor("w2T", [H, D], F32, kind="ExternalInput").ap()
    bp = {}
    for nm, w in (("b_in", KD), ("b_gx", KD), ("b_ga", KD), ("b_out", KD)):
        bp[nm] = nc.dram_tensor(nm + "_p", [128, w], F32, kind="ExternalInput").ap()
    biasp = nc.dram_tensor("bias_pack", [128, 512], F32, kind="ExternalInput").ap()
    maskT = nc.dram_tensor("maskT", [D, S], FP8, kind="ExternalInput").ap()
    outT = nc.dram_tensor("outT", [D, S], F32, kind="ExternalOutput").ap()

    import contextlib
    stack = contextlib.ExitStack()
    T = lambda name, shape, dt: stack.enter_context(nc.sbuf_tensor(name, shape, dt))

    wattn = T("wattn", [128, 4 * KD * D], F16)
    w1q = T("w1q", [128, KD * H], FP8)
    w2q = T("w2q", [128, KH * D], FP8)
    wstage = T("wstage", [128, 512], F32)
    bt = {nm: T("bt_" + nm, [128, KD], F32) for nm in ("b_in", "b_gx", "b_ga", "b_out")}
    b2q = T("b2q", [128, KD], F32)
    b1q = T("b1q", [128, KH], F32)
    cp = T("cp", [128, KD], F32)

    xs16 = T("xs16", [128, KD * NT], F16)
    AS = xs16
    x1s = T("x1s", [128, KD * NT], F16)
    GA = T("GA", [128, KD * NT], F16)
    HS = GA
    GX = T("GX", [128, KD * NT], F16)
    BS = GX
    r16 = T("r16", [128, KD * NT], F16)
    hq8 = T("hq8", [128, KH * NT], FP8)
    x2q8 = T("x2q8", [128, KD * NT], FP8)
    maskb = T("maskb", [128, 2 * NT], FP8)
    xstage = T("xstage", [128, NT], F32)
    outst = T("outst", [128, 2 * NT], F32)
    ti32 = T("ti32", [128, 512], I32) if SIMQ else None
    t16a = T("t16a", [128, NT], F16)
    t8a = T("t8a", [128, 512], FP8)
    ut1 = T("ut1", [128, NT], F32)
    ut2 = T("ut2", [128, NT], F32)
    tb16 = T("tb16", [128, NT], BF16)
    tq16 = T("tq16", [128, NT], BF16)
    carry = T("carry", [128, KD], F32)
    ones_bb = T("ones_bb", [128, 128], BF16)
    ones_hh = T("ones_hh", [128, 128], F16)
    M32 = wstage
    S32 = T("S32", [128, NT], F32)
    cst = T("cst", [128, 3], F32)

    P = lambda name: stack.enter_context(nc.psum_tensor(name, [128, NT], F32))
    pb = [P("pb0"), P("pb1"), P("pb2"), P("pb3")]
    pstat_s1 = P("pstat_s1")
    pstat_q1 = P("pstat_q1")
    pstat_s2 = P("pstat_s2")
    pstat_q2 = P("pstat_q2")

    SEM = {k: stack.enter_context(nc.semaphore("sem_" + k)) for k in ENG}
    NDS = 4
    for q in ("dy", "dg"):
        for i in range(NDS):
            SEM[q + str(i)] = stack.enter_context(nc.semaphore("sem_" + q + str(i)))

    counts = {k: 0 for k in SEM}
    dma_idx = {"dy": 0, "dg": 0}
    OPS = []

    def rec(ek, fn):
        OPS.append((ek, fn))

    def bump(ek):
        counts[ek] += 1
        return counts[ek]

    def op(ek, emit, waits=(), inc=False):
        w = tuple(waits)
        if inc:
            counts[ek] += 1

        def fn(eng, emit=emit, w=w, inc=inc):
            for sk, val in w:
                eng.wait_ge(SEM[sk], val)
            ins = emit(eng)
            if inc:
                ins.then_inc(SEM[ek], 1)
        rec(ek, fn)
        return counts[ek]

    def dma(qk, ek, out_ap, in_ap, waits=()):
        sk = qk + str(dma_idx[qk] % NDS)
        dma_idx[qk] += 1
        w = list(waits)
        if counts[sk] > 0:
            w.append((sk, counts[sk]))
        counts[sk] += 16
        tgt = counts[sk]
        w = tuple(w)

        def fn(eng, out_ap=out_ap, in_ap=in_ap, w=w, sk=sk):
            for k2, val in w:
                eng.wait_ge(SEM[k2], val)
            eng.dma_start(out=out_ap, in_=in_ap).then_inc(SEM[sk], 16)
        rec(ek, fn)
        return (sk, tgt)

    def barrier():
        vals = {}
        for k in ENG:
            counts[k] += 1
            vals[k] = counts[k]
        for k in ENG:
            def fn(eng, k=k, vals=dict(vals)):
                eng.sem_inc(SEM[k], 1)
                for k2, v in vals.items():
                    if k2 != k:
                        eng.wait_ge(SEM[k2], v)
            rec(k, fn)

    def sl(t_, j, w=NT):
        return t_[:, j * w:(j + 1) * w]

    def q_sim(dst_ap, src_ap, m_bits, min_e, waits=(), inc=False):
        phi = float(2.0 ** (52 - m_bits) + 1.0)
        Mg = float(1.5 * 2.0 ** (52 + min_e - m_bits))
        thr = float(2.0 ** min_e)
        A = ut2[:]
        Bb = ti32[:, 0:512]
        op("v", lambda e: e.tensor_scalar(out=A, in0=src_ap, scalar1=phi, scalar2=None, op0=OP.mult), waits=waits)
        op("v", lambda e: e.tensor_sub(out=Bb, in0=A, in1=src_ap))
        op("v", lambda e: e.tensor_sub(out=A, in0=A, in1=Bb))
        op("v", lambda e: e.tensor_scalar(out=Bb, in0=src_ap, scalar1=Mg, scalar2=None, op0=OP.add))
        op("v", lambda e: e.tensor_scalar(out=Bb, in0=Bb, scalar1=-Mg, scalar2=None, op0=OP.add))
        op("v", lambda e: e.tensor_scalar(out=dst_ap, in0=src_ap, scalar1=-1.0, scalar2=None, op0=OP.mult))
        op("v", lambda e: e.tensor_max(out=dst_ap, in0=dst_ap, in1=src_ap))
        op("v", lambda e: e.tensor_scalar(out=dst_ap, in0=dst_ap, scalar1=thr, scalar2=None, op0=OP.is_lt))
        op("v", lambda e: e.tensor_sub(out=Bb, in0=Bb, in1=A))
        op("v", lambda e: e.tensor_mul(out=Bb, in0=Bb, in1=dst_ap))
        return op("v", lambda e: e.tensor_add(out=dst_ap, in0=A, in1=Bb), inc=inc)

    watile = lambda widx, k, j: wattn[:, widx * KD * D + k * D + j * 128: widx * KD * D + k * D + j * 128 + 128]
    w1tile = lambda k, hj: w1q[:, k * H + hj * 128: k * H + hj * 128 + 128]
    w2tile = lambda hj, j: w2q[:, hj * D + j * 128: hj * D + j * 128 + 128]

    op("v", lambda e: e.memset(ones_bb[:], 1.0))
    op("v", lambda e: e.memset(ones_hh[:], 1.0))
    op("v", lambda e: e.memset(carry[:], 0.0))
    op("v", lambda e: e.memset(cst[:, 0:1], 0.5))
    op("v", lambda e: e.memset(cst[:, 1:2], -0.5))
    op("v", lambda e: e.memset(cst[:, 2:3], 1e-5))

    for nm in ("b_in", "b_gx", "b_ga", "b_out"):
        dma("dy", "y", bt[nm][:], bp[nm])
    t_bp = dma("dy", "y", S32[:], biasp)

    op("s", lambda e: e.activation(out=cp[:], in_=S32[:, 40:48], func=AF.Exp), waits=[t_bp])
    cs = op("s", lambda e: e.activation(out=cp[:], in_=cp[:], func=AF.Ln, bias=1.0), inc=True)
    cs = op("s", lambda e: e.activation(out=cp[:], in_=cp[:], func=AF.Identity, scale=-16.0), inc=True)

    if SIMQ:
        q_sim(ut1[:], S32[:], 3, -6, waits=[t_bp])
    else:
        op("v", lambda e: e.tensor_copy(out=t8a[:, 0:512], in_=S32[:]), waits=[t_bp])
        op("v", lambda e: e.tensor_copy(out=ut1[:], in_=t8a[:, 0:512]))
    op("v", lambda e: e.tensor_copy(out=b2q[:], in_=ut1[:, 0:KD]))
    op("v", lambda e: e.tensor_copy(out=b1q[:], in_=ut1[:, KD:KD + KH]), inc=True)

    for widx, nm in enumerate(("W_in", "W_gx", "W_ga", "W_out")):
        for k in range(KD):
            for h in range(2):
                vprev = counts["v"]
                tl = dma("dg", "g", wstage[:],
                         Wat[nm][k * 128:(k + 1) * 128, h * 512:(h + 1) * 512],
                         waits=[("v", vprev)])
                off = widx * KD * D + k * D + h * 512
                op("v", lambda e, off=off: e.tensor_copy(
                    out=wattn[:, off:off + 512], in_=wstage[:]),
                    waits=[tl], inc=True)
    def qz8_big(dst_ap, tl):
        if SIMQ:
            return q_sim(dst_ap, wstage[:], 3, -6, waits=[tl], inc=True)
        return op("v", lambda e, dst_ap=dst_ap: e.tensor_copy(out=dst_ap, in_=wstage[:]), waits=[tl], inc=True)

    for k in range(KD):
        for r in range(H // 512):
            vprev = counts["v"]
            tl = dma("dg", "g", wstage[:], w1T[k * 128:(k + 1) * 128, r * 512:(r + 1) * 512],
                     waits=[("v", vprev)])
            qz8_big(w1q[:, k * H + r * 512: k * H + (r + 1) * 512], tl)
    for hk in range(KH):
        for r in range(2):
            vprev = counts["v"]
            tl = dma("dg", "g", wstage[:], w2T[hk * 128:(hk + 1) * 128, r * 512:(r + 1) * 512],
                     waits=[("v", vprev)])
            qz8_big(w2q[:, hk * D + r * 512: hk * D + (r + 1) * 512], tl)

    barrier()


    NB = 4
    bank_free = [0] * NB
    bank_free_ek = ["v"] * NB
    gbank = [0]

    def next_bank():
        bk = gbank[0] % NB
        gbank[0] += 1
        return bk

    last_stat_t = [0]
    last_vti = [0]
    out_dma_t = []
    mask_hist = []
    st_ = {"vscan": 0, "vx2": 0, "vh": 0, "tstat1": 0, "tstat2": 0,
           "xs_ready": [0] * KD, "x1_ready": [0] * KD,
           "ev1": [0] * KD, "ev2": [0] * KD, "out_g": 0, "xst": None}

    def ln_stats_tile(j, src_fn, vready, ps, pq):
        swts = [("v", vready)]
        if last_stat_t[0]:
            swts.append(("t", last_stat_t[0]))
        c2 = op("s", lambda e, j=j: e.activation(
            out=tq16[:], in_=src_fn(j), func=AF.Square), waits=swts, inc=True)
        op("t", lambda e, j=j, ps=ps: e.matmul(
            out=ps[:], lhsT=ones_hh[:], rhs=src_fn(j),
            start=(j == 0), stop=(j == KD - 1)), waits=[("v", vready)])
        last_stat_t[0] = op("t", lambda e, j=j, pq=pq: e.matmul(
            out=pq[:], lhsT=ones_bb[:], rhs=tq16[:],
            start=(j == 0), stop=(j == KD - 1)), waits=[("s", c2)], inc=True)

    def ln_finalize(tstat_val, ps, pq, gw=0):
        fw = [("t", tstat_val)]
        if gw:
            fw.append(("g", gw))
        op("v", lambda e, ps=ps: e.tensor_scalar(
            out=M32[:], in0=ps[:], scalar1=1.0 / D, scalar2=None, op0=OP.mult),
            waits=fw)
        op("v", lambda e, pq=pq: e.tensor_scalar(
            out=S32[:], in0=pq[:], scalar1=1.0 / D, scalar2=None, op0=OP.mult))
        op("v", lambda e: e.tensor_mul(out=ut1[:], in0=M32[:], in1=M32[:]))
        op("v", lambda e: e.tensor_sub(out=ut1[:], in0=S32[:], in1=ut1[:]))
        vv = op("v", lambda e: e.drain(), inc=True)
        op("s", lambda e: e.activation(out=ut1[:], in_=ut1[:], func=AF.Ln, bias=cst[:, 2:3]),
           waits=[("v", vv)])
        op("s", lambda e: e.activation(out=S32[:], in_=ut1[:], func=AF.Exp, scale=cst[:, 1:2]))
        return op("s", lambda e: e.drain(), inc=True)

    def emit_outproj(b):
        t0 = b * NT
        for j in range(KD):
            bank = next_bank()
            tl = dma("dy", "y", xstage[:], xT[j * 128:(j + 1) * 128, t0:t0 + NT],
                     waits=[st_["xst"]] if st_["xst"] else [])
            for k in range(KD):
                wts = [("v", st_["vscan"])] if k == 0 else []
                if k == 0 and bank_free[bank]:
                    wts.append((bank_free_ek[bank], bank_free[bank]))
                op("t", lambda e, j=j, k=k, bank=bank: e.matmul(
                    out=pb[bank][:], lhsT=watile(3, k, j), rhs=sl(HS, k),
                    start=(k == 0), stop=(k == KD - 1)),
                    waits=wts, inc=(k == KD - 1))
            tmm = counts["t"]
            swts = [("t", tmm), tl]
            if j == 0 and st_["out_g"]:
                swts.append(("g", st_["out_g"]))
            ev = op("v", lambda e, j=j, bank=bank: e.scalar_tensor_tensor(
                out=sl(r16, j), in0=pb[bank][:], scalar=bt["b_out"][:, j:j + 1],
                in1=xstage[:], op0=OP.add, op1=OP.add),
                waits=swts, inc=True)
            st_["xst"] = ("v", ev)
            bank_free[bank] = ev
            bank_free_ek[bank] = "v"
            st_["ev1"][j] = ev
            if j > 1:
                ln_stats_tile(j - 2, lambda jj: sl(r16, jj), st_["ev1"][j - 2],
                              pstat_s1, pstat_q1)
        ln_stats_tile(KD - 2, lambda jj: sl(r16, jj), st_["ev1"][KD - 2],
                      pstat_s1, pstat_q1)
        ln_stats_tile(KD - 1, lambda jj: sl(r16, jj), st_["ev1"][KD - 1],
                      pstat_s1, pstat_q1)
        st_["tstat1"] = counts["t"]

    def emit_P1_loads(b):
        t0 = b * NT
        for k in range(KD):
            tl = dma("dy", "y", xstage[:], xT[k * 128:(k + 1) * 128, t0:t0 + NT],
                     waits=[st_["xst"]] if st_["xst"] else [])
            cw = [tl]
            if k == 0:
                cw.append(("v", st_["vscan"]))
            c = op("g", lambda e, k=k: e.tensor_copy(
                out=sl(xs16, k), in_=xstage[:]), waits=cw, inc=True)
            st_["xs_ready"][k] = c
            st_["xst"] = ("g", c)

    def emit_P1_groups(b):
        for j in range(KD):
            bank = next_bank()
            for k in range(KD):
                wts = [("g", st_["xs_ready"][k])]
                if k == 0 and bank_free[bank]:
                    wts.append((bank_free_ek[bank], bank_free[bank]))
                op("t", lambda e, j=j, k=k, bank=bank: e.matmul(
                    out=pb[bank][:], lhsT=watile(0, k, j), rhs=sl(xs16, k),
                    start=(k == 0), stop=(k == KD - 1)),
                    waits=wts, inc=(k == KD - 1))
            tmm = counts["t"]
            ev = op("s", lambda e, j=j, bank=bank: e.activation(
                out=sl(x1s, j), in_=pb[bank][:], func=AF.Identity,
                bias=bt["b_in"][:, j:j + 1], scale=1.0),
                waits=[("t", tmm)], inc=True)
            st_["x1_ready"][j] = ev
            bank_free[bank] = ev
            bank_free_ek[bank] = "s"

    def emit_gate_group(j, widx, dstb, bias_t):
        bank = next_bank()
        for k in range(KD):
            wts = [("s", st_["x1_ready"][k])]
            if k == 0 and bank_free[bank]:
                wts.append((bank_free_ek[bank], bank_free[bank]))
            op("t", lambda e, j=j, k=k, bank=bank, widx=widx: e.matmul(
                out=pb[bank][:], lhsT=watile(widx, k, j), rhs=sl(x1s, k),
                start=(k == 0), stop=(k == KD - 1)),
                waits=wts, inc=(k == KD - 1))
        tmm = counts["t"]
        ev = op("s", lambda e, j=j, dstb=dstb, bias_t=bias_t, bank=bank: e.activation(
            out=sl(dstb, j), in_=pb[bank][:], func=AF.Sigmoid,
            bias=bt[bias_t][:, j:j + 1], scale=1.0),
            waits=[("t", tmm)], inc=True)
        bank_free[bank] = ev
        bank_free_ek[bank] = "s"

    def emit_P2e_scan(b):
        for j in range(KD):
            op("s", lambda e, j=j: e.activation(
                out=sl(AS, j), in_=sl(GA, j), func=AF.Exp,
                bias=0.0, scale=cp[:, j:j + 1]))
        for j in range(KD):
            op("s", lambda e, j=j: e.activation(
                out=sl(GA, j), in_=sl(AS, j), func=AF.Sqrt, bias=1.0, scale=-1.0))
        for j in range(KD):
            op("s", lambda e, j=j: e.activation(
                out=sl(AS, j), in_=sl(AS, j), func=AF.Sqrt))
        sad = op("s", lambda e: e.drain(), inc=True)
        for j in range(KD):
            op("v", lambda e, j=j: e.tensor_mul(out=ut2[:], in0=sl(GX, j), in1=sl(x1s, j)),
               waits=[("s", sad)] if j == 0 else ())
            op("v", lambda e, j=j: e.tensor_mul(out=sl(BS, j), in0=ut2[:], in1=sl(GA, j)), inc=True)
        for j in range(KD):
            sc_v = op("v", lambda e, j=j: e.tensor_tensor_scan(
                out=sl(HS, j), data0=sl(AS, j), data1=sl(BS, j),
                initial=carry[:, j:j + 1], op0=OP.mult, op1=OP.add), inc=True)
            op("v", lambda e, j=j: e.tensor_copy(
                out=carry[:, j:j + 1], in_=HS[:, j * NT + NT - 1: j * NT + NT]),
                waits=[("v", sc_v)], inc=True)
        st_["vscan"] = counts["v"]

    def emit_LN1_x2(b):
        tbc1 = ln_finalize(st_["tstat1"], pstat_s1, pstat_q1, gw=st_["out_g"])
        for j in range(KD):
            op("v", lambda e, j=j: e.tensor_sub(out=sl(r16, j), in0=sl(r16, j), in1=M32[:]),
               waits=[("s", tbc1)] if j == 0 else ())
            op("v", lambda e, j=j: e.tensor_mul(out=ut1[:], in0=sl(r16, j), in1=S32[:]))
            if SIMQ:
                q_sim(sl(x2q8, j), ut1[:], 3, -6, inc=True)
            else:
                op("v", lambda e, j=j: e.tensor_copy(out=sl(x2q8, j), in_=ut1[:]), inc=True)
        st_["vx2"] = counts["v"]

    def emit_ffn1_group(b, hj):
        bank = next_bank()
        for k in range(KD):
            wts = [("v", st_["vx2"])] if k == 0 else []
            if k == 0 and bank_free[bank]:
                wts.append((bank_free_ek[bank], bank_free[bank]))
            op("t", lambda e, hj=hj, k=k, bank=bank: e.matmul(
                out=pb[bank][:], lhsT=w1tile(k, hj), rhs=sl(x2q8, k),
                start=(k == 0), stop=(k == KD - 1)),
                waits=wts, inc=(k == KD - 1))
        tmm = counts["t"]
        sev = op("s", lambda e, hj=hj, bank=bank: e.activation(
            out=t16a[:], in_=pb[bank][:], func=AF.Relu,
            bias=b1q[:, hj:hj + 1], scale=1.0),
            waits=[("t", tmm), ("v", last_vti[0])], inc=True)
        if SIMQ:
            last_vti[0] = q_sim(ut1[:], t16a[:], 10, -14, waits=[("s", sev)], inc=True)
            ev = q_sim(sl(hq8, hj), ut1[:], 3, -6, inc=True)
        else:
            ev = op("v", lambda e, hj=hj: e.tensor_copy(
                out=sl(hq8, hj), in_=t16a[:]), waits=[("s", sev)], inc=True)
            last_vti[0] = ev
        bank_free[bank] = ev
        bank_free_ek[bank] = "v"

    def emit_ffn2_unit(b, j):
        t0 = b * NT
        slot = len(mask_hist) % 2
        mwts = []
        if len(mask_hist) >= 2:
            mwts.append(("v", mask_hist[-2]))
        mt = dma("dy", "y", maskb[:, slot * NT:(slot + 1) * NT],
                 maskT[j * 128:(j + 1) * 128, t0:t0 + NT], waits=mwts)
        bank = next_bank()
        for hj in range(KH):
            wts = [("v", st_["vh"])] if hj == 0 else []
            if hj == 0 and bank_free[bank]:
                wts.append((bank_free_ek[bank], bank_free[bank]))
            op("t", lambda e, j=j, hj=hj, bank=bank: e.matmul(
                out=pb[bank][:], lhsT=w2tile(hj, j), rhs=sl(hq8, hj),
                start=(hj == 0), stop=(hj == KH - 1)),
                waits=wts, inc=(hj == KH - 1))
        tmm = counts["t"]
        sev = op("s", lambda e, j=j, bank=bank: e.activation(
            out=t16a[:], in_=pb[bank][:], func=AF.Identity,
            bias=b2q[:, j:j + 1], scale=1.0),
            waits=[("t", tmm), ("v", last_vti[0])], inc=True)
        if SIMQ:
            last_vti[0] = q_sim(ut1[:], t16a[:], 10, -14,
                                waits=[("s", sev), mt], inc=True)
        else:
            last_vti[0] = op("v", lambda e: e.tensor_copy(out=ut1[:], in_=t16a[:]),
                             waits=[("s", sev), mt], inc=True)
        mr = op("v", lambda e, slot=slot: e.tensor_mul(
            out=ut1[:], in0=ut1[:], in1=maskb[:, slot * NT:(slot + 1) * NT]), inc=True)
        mask_hist.append(mr)
        op("v", lambda e, j=j: e.tensor_mul(out=ut2[:], in0=sl(r16, j), in1=S32[:]))
        ev = op("v", lambda e, j=j: e.tensor_add(out=sl(r16, j), in0=ut1[:], in1=ut2[:]),
                inc=True)
        bank_free[bank] = ev
        bank_free_ek[bank] = "v"
        st_["ev2"][j] = ev
        if j > 1:
            ln_stats_tile(j - 2, lambda jj: sl(r16, jj), st_["ev2"][j - 2],
                          pstat_s2, pstat_q2)

    def emit_LN2_out(b):
        t0 = b * NT
        st_["tstat2"] = counts["t"]
        tbc2 = ln_finalize(st_["tstat2"], pstat_s2, pstat_q2)
        for j in range(KD):
            oslot = j % 2
            wts = [("s", tbc2)] if j == 0 else []
            if len(out_dma_t) >= 2:
                wts.append(out_dma_t[-2])
            osl = outst[:, oslot * NT:(oslot + 1) * NT]
            op("v", lambda e, j=j: e.tensor_sub(out=ut1[:], in0=sl(r16, j), in1=M32[:]),
               waits=wts)
            ev = op("v", lambda e, osl=osl: e.tensor_mul(out=osl, in0=ut1[:], in1=S32[:]),
                    inc=True)
            tl = dma("dg", "g", outT[j * 128:(j + 1) * 128, t0:t0 + NT],
                     osl, waits=[("v", ev)])
            out_dma_t.append(tl)

    for blk in range(NBLK):
        if blk == 0:
            emit_P1_loads(0)
            emit_P1_groups(0)
            for j in range(KD):
                emit_gate_group(j, 2, GA, "b_ga")
            for j in range(KD):
                emit_gate_group(j, 1, GX, "b_gx")
            emit_P2e_scan(0)
        emit_outproj(blk)
        if blk + 1 < NBLK:
            emit_P1_loads(blk + 1)
        emit_LN1_x2(blk)
        if blk + 1 < NBLK:
            emit_P1_groups(blk + 1)
        gate_units = ([(j, 2, GA, "b_ga") for j in range(KD)] +
                      [(j, 1, GX, "b_gx") for j in range(KD)]) if blk + 1 < NBLK else []
        gi = 0
        for hj in range(KH):
            emit_ffn1_group(blk, hj)
            if hj % 2 == 1 and gi < len(gate_units):
                emit_gate_group(*gate_units[gi])
                gi += 1
        while gi < len(gate_units):
            emit_gate_group(*gate_units[gi])
            gi += 1
        st_["vh"] = counts["v"]
        for j in range(KD):
            emit_ffn2_unit(blk, j)
            if j == 1 and blk + 1 < NBLK:
                emit_P2e_scan(blk + 1)
        ln_stats_tile(KD - 2, lambda jj: sl(r16, jj), st_["ev2"][KD - 2],
                      pstat_s2, pstat_q2)
        ln_stats_tile(KD - 1, lambda jj: sl(r16, jj), st_["ev2"][KD - 1],
                      pstat_s2, pstat_q2)
        emit_LN2_out(blk)

    finals = [(q + str(i), counts[q + str(i)]) for q in ("dy", "dg") for i in range(NDS)
              if counts[q + str(i)] > 0]

    def _final(eng, finals=tuple(finals)):
        for sk, val in finals:
            eng.wait_ge(SEM[sk], val)
    rec("g", _final)

    blk_ctx = stack.enter_context(nc.Block())

    def run_engine(ek):
        def section(eng):
            for k, fn in OPS:
                if k == ek:
                    fn(eng)
        return section

    blk_ctx.tensor(run_engine("t"))
    blk_ctx.vector(run_engine("v"))
    blk_ctx.scalar(run_engine("s"))
    blk_ctx.gpsimd(run_engine("g"))
    blk_ctx.sync(run_engine("y"))

    stack.close()
    return nc


def _make_mask():
    import jax
    cpu = jax.devices("cpu")[0]
    shape = (B, S, D)
    if os.environ.get("BASS_SIM_FP64") == "1":
        prev = jax.config.jax_enable_x64
        jax.config.update("jax_enable_x64", True)
        try:
            with jax.default_device(cpu):
                keep = np.asarray(jax.random.bernoulli(
                    jax.random.key(42, impl="threefry2x32"), 1.0 - 0.2, shape))
        finally:
            jax.config.update("jax_enable_x64", prev)
    else:
        with jax.default_device(cpu):
            keep = np.asarray(jax.random.bernoulli(
                jax.random.key(42, impl="threefry2x32"), 1.0 - 0.2, shape))
    return np.where(keep, np.float32(1.25), np.float32(0.0)).astype(ml_dtypes.float8_e4m3)


def kernel(L0_attn_x, L0_attn_W_in, L0_attn_b_in, L0_attn_W_gx, L0_attn_b_gx,
           L0_attn_W_ga, L0_attn_b_ga, L0_attn_a_param, L0_attn_W_out, L0_attn_b_out,
           L0_ffn_w1, L0_ffn_b1, L0_ffn_w2, L0_ffn_b2):
    x = np.asarray(L0_attn_x, np.float32)
    maskv = _make_mask()

    packKD = lambda v: np.ascontiguousarray(np.asarray(v, np.float32).reshape(KD, 128).T)
    shared = {
        "W_in": np.ascontiguousarray(np.asarray(L0_attn_W_in, np.float32)),
        "W_gx": np.ascontiguousarray(np.asarray(L0_attn_W_gx, np.float32)),
        "W_ga": np.ascontiguousarray(np.asarray(L0_attn_W_ga, np.float32)),
        "W_out": np.ascontiguousarray(np.asarray(L0_attn_W_out, np.float32)),
        "w1T": np.ascontiguousarray(np.asarray(L0_ffn_w1, np.float32).T),
        "w2T": np.ascontiguousarray(np.asarray(L0_ffn_w2, np.float32).T),
        "b_in_p": packKD(L0_attn_b_in),
        "b_gx_p": packKD(L0_attn_b_gx),
        "b_ga_p": packKD(L0_attn_b_ga),
        "b_out_p": packKD(L0_attn_b_out),
    }
    bias_pack = np.zeros((128, 512), np.float32)
    bias_pack[:, 0:8] = packKD(L0_ffn_b2)
    bias_pack[:, 8:40] = np.asarray(L0_ffn_b1, np.float32).reshape(32, 128).T
    bias_pack[:, 40:48] = packKD(L0_attn_a_param)
    shared["bias_pack"] = bias_pack
    in_maps = []
    for c in range(B):
        m = dict(shared)
        m["xT"] = np.ascontiguousarray(x[c].T)
        m["maskT"] = np.ascontiguousarray(maskv[c].T)
        in_maps.append(m)

    nc = build_program()
    res = run_bass_kernel_spmd(nc, in_maps, core_ids=list(range(B)))
    out = np.empty((B, S, D), np.float32)
    for c in range(B):
        out[c] = np.asarray(res.results[c]["outT"]).T
    return out


# version 12
# speedup vs baseline: 1.0490x; 1.0490x over previous
import os
import math
import numpy as np
import ml_dtypes
import concourse.bass as bass
import concourse.mybir as mybir
from concourse.bass_utils import run_bass_kernel_spmd

F32 = mybir.dt.float32
F16 = mybir.dt.float16
BF16 = mybir.dt.bfloat16
FP8 = mybir.dt.float8e4
I32 = mybir.dt.int32
AF = mybir.ActivationFunctionType
OP = mybir.AluOpType

SIMQ = os.environ.get("BASS_SIM_FP64") == "1"

B, S, D, H = 8, 4096, 1024, 4096
NT = 512
NBLK = S // NT
KD = D // 128
KH = H // 128
LN2C = 0.6931471805599453

ENG = ("t", "v", "s", "g", "y")


def build_program():
    nc = bass.Bass("TRN2", target_bir_lowering=False, debug=False, detect_race_conditions=False)

    xT = nc.dram_tensor("xT", [D, S], F32, kind="ExternalInput").ap()
    Wat = {}
    for nm in ("W_in", "W_gx", "W_ga", "W_out"):
        Wat[nm] = nc.dram_tensor(nm, [D, D], F32, kind="ExternalInput").ap()
    w1T = nc.dram_tensor("w1T", [D, H], F32, kind="ExternalInput").ap()
    w2T = nc.dram_tensor("w2T", [H, D], F32, kind="ExternalInput").ap()
    bp = {}
    for nm, w in (("b_in", KD), ("b_gx", KD), ("b_ga", KD), ("b_out", KD)):
        bp[nm] = nc.dram_tensor(nm + "_p", [128, w], F32, kind="ExternalInput").ap()
    biasp = nc.dram_tensor("bias_pack", [128, 512], F32, kind="ExternalInput").ap()
    maskT = nc.dram_tensor("maskT", [D, S], FP8, kind="ExternalInput").ap()
    outT = nc.dram_tensor("outT", [D, S], F32, kind="ExternalOutput").ap()

    import contextlib
    stack = contextlib.ExitStack()
    T = lambda name, shape, dt: stack.enter_context(nc.sbuf_tensor(name, shape, dt))

    wattn = T("wattn", [128, 4 * KD * D], F16)
    w1q = T("w1q", [128, KD * H], FP8)
    w2q = T("w2q", [128, KH * D], FP8)
    wstage = T("wstage", [128, 512], F32)
    bt = {nm: T("bt_" + nm, [128, KD], F32) for nm in ("b_in", "b_gx", "b_ga", "b_out")}
    b2q = T("b2q", [128, KD], F32)
    b1q = T("b1q", [128, KH], F32)
    cp = T("cp", [128, KD], F32)

    xs16 = T("xs16", [128, KD * NT], F16)
    AS = xs16
    x1s = T("x1s", [128, KD * NT], F16)
    GA = T("GA", [128, KD * NT], F16)
    HS = GA
    GX = T("GX", [128, KD * NT], F16)
    BS = GX
    r16 = T("r16", [128, KD * NT], F16)
    hq8 = T("hq8", [128, KH * NT], FP8)
    x2q8 = T("x2q8", [128, KD * NT], FP8)
    maskb = T("maskb", [128, 2 * NT], FP8)
    xstage = T("xstage", [128, NT], F32)
    outst = T("outst", [128, 2 * NT], F32)
    ti32 = T("ti32", [128, 512], I32) if SIMQ else None
    t16a = T("t16a", [128, NT], F16)
    t8a = T("t8a", [128, 512], FP8)
    ut1 = T("ut1", [128, NT], F32)
    ut2 = T("ut2", [128, NT], F32)
    tb16 = T("tb16", [128, NT], BF16)
    tq16 = T("tq16", [128, NT], BF16)
    carry = T("carry", [128, KD], F32)
    ones_bb = T("ones_bb", [128, 128], BF16)
    ones_hh = T("ones_hh", [128, 128], F16)
    M32 = wstage
    S32 = T("S32", [128, NT], F32)
    cst = T("cst", [128, 3], F32)

    P = lambda name: stack.enter_context(nc.psum_tensor(name, [128, NT], F32))
    pb = [P("pb0"), P("pb1"), P("pb2"), P("pb3")]
    pstat_s1 = P("pstat_s1")
    pstat_q1 = P("pstat_q1")
    pstat_s2 = P("pstat_s2")
    pstat_q2 = P("pstat_q2")

    SEM = {k: stack.enter_context(nc.semaphore("sem_" + k)) for k in ENG}
    NDS = 4
    for q in ("dy", "dg"):
        for i in range(NDS):
            SEM[q + str(i)] = stack.enter_context(nc.semaphore("sem_" + q + str(i)))

    counts = {k: 0 for k in SEM}
    dma_idx = {"dy": 0, "dg": 0}
    OPS = []

    def rec(ek, fn):
        OPS.append((ek, fn))

    def bump(ek):
        counts[ek] += 1
        return counts[ek]

    def op(ek, emit, waits=(), inc=False):
        w = tuple(waits)
        if inc:
            counts[ek] += 1

        def fn(eng, emit=emit, w=w, inc=inc):
            for sk, val in w:
                eng.wait_ge(SEM[sk], val)
            ins = emit(eng)
            if inc:
                ins.then_inc(SEM[ek], 1)
        rec(ek, fn)
        return counts[ek]

    def dma(qk, ek, out_ap, in_ap, waits=()):
        sk = qk + str(dma_idx[qk] % NDS)
        dma_idx[qk] += 1
        w = list(waits)
        if counts[sk] > 0:
            w.append((sk, counts[sk]))
        counts[sk] += 16
        tgt = counts[sk]
        w = tuple(w)

        def fn(eng, out_ap=out_ap, in_ap=in_ap, w=w, sk=sk):
            for k2, val in w:
                eng.wait_ge(SEM[k2], val)
            eng.dma_start(out=out_ap, in_=in_ap).then_inc(SEM[sk], 16)
        rec(ek, fn)
        return (sk, tgt)

    def barrier():
        vals = {}
        for k in ENG:
            counts[k] += 1
            vals[k] = counts[k]
        for k in ENG:
            def fn(eng, k=k, vals=dict(vals)):
                eng.sem_inc(SEM[k], 1)
                for k2, v in vals.items():
                    if k2 != k:
                        eng.wait_ge(SEM[k2], v)
            rec(k, fn)

    def sl(t_, j, w=NT):
        return t_[:, j * w:(j + 1) * w]

    def q_sim(dst_ap, src_ap, m_bits, min_e, waits=(), inc=False):
        phi = float(2.0 ** (52 - m_bits) + 1.0)
        Mg = float(1.5 * 2.0 ** (52 + min_e - m_bits))
        thr = float(2.0 ** min_e)
        A = ut2[:]
        Bb = ti32[:, 0:512]
        op("v", lambda e: e.tensor_scalar(out=A, in0=src_ap, scalar1=phi, scalar2=None, op0=OP.mult), waits=waits)
        op("v", lambda e: e.tensor_sub(out=Bb, in0=A, in1=src_ap))
        op("v", lambda e: e.tensor_sub(out=A, in0=A, in1=Bb))
        op("v", lambda e: e.tensor_scalar(out=Bb, in0=src_ap, scalar1=Mg, scalar2=None, op0=OP.add))
        op("v", lambda e: e.tensor_scalar(out=Bb, in0=Bb, scalar1=-Mg, scalar2=None, op0=OP.add))
        op("v", lambda e: e.tensor_scalar(out=dst_ap, in0=src_ap, scalar1=-1.0, scalar2=None, op0=OP.mult))
        op("v", lambda e: e.tensor_max(out=dst_ap, in0=dst_ap, in1=src_ap))
        op("v", lambda e: e.tensor_scalar(out=dst_ap, in0=dst_ap, scalar1=thr, scalar2=None, op0=OP.is_lt))
        op("v", lambda e: e.tensor_sub(out=Bb, in0=Bb, in1=A))
        op("v", lambda e: e.tensor_mul(out=Bb, in0=Bb, in1=dst_ap))
        return op("v", lambda e: e.tensor_add(out=dst_ap, in0=A, in1=Bb), inc=inc)

    watile = lambda widx, k, j: wattn[:, widx * KD * D + k * D + j * 128: widx * KD * D + k * D + j * 128 + 128]
    w1tile = lambda k, hj: w1q[:, k * H + hj * 128: k * H + hj * 128 + 128]
    w2tile = lambda hj, j: w2q[:, hj * D + j * 128: hj * D + j * 128 + 128]

    op("v", lambda e: e.memset(ones_bb[:], 1.0))
    op("v", lambda e: e.memset(ones_hh[:], 1.0))
    op("v", lambda e: e.memset(carry[:], 0.0))
    op("v", lambda e: e.memset(cst[:, 0:1], 0.5))
    op("v", lambda e: e.memset(cst[:, 1:2], -0.5))
    op("v", lambda e: e.memset(cst[:, 2:3], 1e-5))

    for nm in ("b_in", "b_gx", "b_ga", "b_out"):
        dma("dy", "y", bt[nm][:], bp[nm])
    t_bp = dma("dy", "y", S32[:], biasp)

    op("s", lambda e: e.activation(out=cp[:], in_=S32[:, 40:48], func=AF.Exp), waits=[t_bp])
    cs = op("s", lambda e: e.activation(out=cp[:], in_=cp[:], func=AF.Ln, bias=1.0), inc=True)
    cs = op("s", lambda e: e.activation(out=cp[:], in_=cp[:], func=AF.Identity, scale=-16.0), inc=True)

    if SIMQ:
        q_sim(ut1[:], S32[:], 3, -6, waits=[t_bp])
    else:
        op("v", lambda e: e.tensor_copy(out=t8a[:, 0:512], in_=S32[:]), waits=[t_bp])
        op("v", lambda e: e.tensor_copy(out=ut1[:], in_=t8a[:, 0:512]))
    op("v", lambda e: e.tensor_copy(out=b2q[:], in_=ut1[:, 0:KD]))
    op("v", lambda e: e.tensor_copy(out=b1q[:], in_=ut1[:, KD:KD + KH]), inc=True)

    for widx, nm in enumerate(("W_in", "W_gx", "W_ga", "W_out")):
        for k in range(KD):
            for h in range(2):
                vprev = counts["v"]
                tl = dma("dg", "g", wstage[:],
                         Wat[nm][k * 128:(k + 1) * 128, h * 512:(h + 1) * 512],
                         waits=[("v", vprev)])
                off = widx * KD * D + k * D + h * 512
                op("v", lambda e, off=off: e.tensor_copy(
                    out=wattn[:, off:off + 512], in_=wstage[:]),
                    waits=[tl], inc=True)
    def qz8_big(dst_ap, tl):
        if SIMQ:
            return q_sim(dst_ap, wstage[:], 3, -6, waits=[tl], inc=True)
        return op("v", lambda e, dst_ap=dst_ap: e.tensor_copy(out=dst_ap, in_=wstage[:]), waits=[tl], inc=True)

    for k in range(KD):
        for r in range(H // 512):
            vprev = counts["v"]
            tl = dma("dg", "g", wstage[:], w1T[k * 128:(k + 1) * 128, r * 512:(r + 1) * 512],
                     waits=[("v", vprev)])
            qz8_big(w1q[:, k * H + r * 512: k * H + (r + 1) * 512], tl)
    for hk in range(KH):
        for r in range(2):
            vprev = counts["v"]
            tl = dma("dg", "g", wstage[:], w2T[hk * 128:(hk + 1) * 128, r * 512:(r + 1) * 512],
                     waits=[("v", vprev)])
            qz8_big(w2q[:, hk * D + r * 512: hk * D + (r + 1) * 512], tl)

    barrier()


    NB = 4
    bank_free = [0] * NB
    bank_free_ek = ["v"] * NB
    gbank = [0]

    def next_bank():
        bk = gbank[0] % NB
        gbank[0] += 1
        return bk

    last_stat_t = [0]
    last_vti = [0]
    out_dma_t = []
    mask_hist = []
    st_ = {"vscan": 0, "vx2": 0, "vh": 0, "tstat1": 0, "tstat2": 0,
           "xs_ready": [0] * KD, "x1_ready": [0] * KD,
           "ev1": [0] * KD, "ev2": [0] * KD, "out_g": 0, "xst": None}

    def ln_stats_tile(j, src_fn, vready, ps, pq):
        swts = [("v", vready)]
        if last_stat_t[0]:
            swts.append(("t", last_stat_t[0]))
        c2 = op("s", lambda e, j=j: e.activation(
            out=tq16[:], in_=src_fn(j), func=AF.Square), waits=swts, inc=True)
        op("t", lambda e, j=j, ps=ps: e.matmul(
            out=ps[:], lhsT=ones_hh[:], rhs=src_fn(j),
            start=(j == 0), stop=(j == KD - 1)), waits=[("v", vready)])
        last_stat_t[0] = op("t", lambda e, j=j, pq=pq: e.matmul(
            out=pq[:], lhsT=ones_bb[:], rhs=tq16[:],
            start=(j == 0), stop=(j == KD - 1)), waits=[("s", c2)], inc=True)

    def ln_finalize(tstat_val, ps, pq, gw=0):
        fw = [("t", tstat_val)]
        if gw:
            fw.append(("g", gw))
        op("v", lambda e, ps=ps: e.tensor_scalar(
            out=M32[:], in0=ps[:], scalar1=1.0 / D, scalar2=None, op0=OP.mult),
            waits=fw)
        op("v", lambda e, pq=pq: e.tensor_scalar(
            out=S32[:], in0=pq[:], scalar1=1.0 / D, scalar2=None, op0=OP.mult))
        op("v", lambda e: e.tensor_mul(out=ut1[:], in0=M32[:], in1=M32[:]))
        op("v", lambda e: e.tensor_sub(out=ut1[:], in0=S32[:], in1=ut1[:]))
        vv = op("v", lambda e: e.drain(), inc=True)
        op("s", lambda e: e.activation(out=ut1[:], in_=ut1[:], func=AF.Ln, bias=cst[:, 2:3]),
           waits=[("v", vv)])
        op("s", lambda e: e.activation(out=S32[:], in_=ut1[:], func=AF.Exp, scale=cst[:, 1:2]))
        return op("s", lambda e: e.drain(), inc=True)

    def emit_outproj(b):
        t0 = b * NT
        for j in range(KD):
            bank = next_bank()
            tl = dma("dy", "y", xstage[:], xT[j * 128:(j + 1) * 128, t0:t0 + NT],
                     waits=[st_["xst"]] if st_["xst"] else [])
            for k in range(KD):
                wts = [("v", st_["vscan"])] if k == 0 else []
                if k == 0 and bank_free[bank]:
                    wts.append((bank_free_ek[bank], bank_free[bank]))
                op("t", lambda e, j=j, k=k, bank=bank: e.matmul(
                    out=pb[bank][:], lhsT=watile(3, k, j), rhs=sl(HS, k),
                    start=(k == 0), stop=(k == KD - 1)),
                    waits=wts, inc=(k == KD - 1))
            tmm = counts["t"]
            swts = [("t", tmm), tl]
            if j == 0 and st_["out_g"]:
                swts.append(("g", st_["out_g"]))
            ev = op("v", lambda e, j=j, bank=bank: e.scalar_tensor_tensor(
                out=sl(r16, j), in0=pb[bank][:], scalar=bt["b_out"][:, j:j + 1],
                in1=xstage[:], op0=OP.add, op1=OP.add),
                waits=swts, inc=True)
            st_["xst"] = ("v", ev)
            bank_free[bank] = ev
            bank_free_ek[bank] = "v"
            st_["ev1"][j] = ev
            if j > 1:
                ln_stats_tile(j - 2, lambda jj: sl(r16, jj), st_["ev1"][j - 2],
                              pstat_s1, pstat_q1)
        ln_stats_tile(KD - 2, lambda jj: sl(r16, jj), st_["ev1"][KD - 2],
                      pstat_s1, pstat_q1)
        ln_stats_tile(KD - 1, lambda jj: sl(r16, jj), st_["ev1"][KD - 1],
                      pstat_s1, pstat_q1)
        st_["tstat1"] = counts["t"]

    def emit_P1_loads(b):
        t0 = b * NT
        for k in range(KD):
            tl = dma("dy", "y", xstage[:], xT[k * 128:(k + 1) * 128, t0:t0 + NT],
                     waits=[st_["xst"]] if st_["xst"] else [])
            c = op("v", lambda e, k=k: e.tensor_copy(
                out=sl(xs16, k), in_=xstage[:]), waits=[tl], inc=True)
            st_["xs_ready"][k] = c
            st_["xst"] = ("v", c)

    def emit_P1_groups(b):
        for j in range(KD):
            bank = next_bank()
            for k in range(KD):
                wts = [("v", st_["xs_ready"][k])]
                if k == 0 and bank_free[bank]:
                    wts.append((bank_free_ek[bank], bank_free[bank]))
                op("t", lambda e, j=j, k=k, bank=bank: e.matmul(
                    out=pb[bank][:], lhsT=watile(0, k, j), rhs=sl(xs16, k),
                    start=(k == 0), stop=(k == KD - 1)),
                    waits=wts, inc=(k == KD - 1))
            tmm = counts["t"]
            ev = op("s", lambda e, j=j, bank=bank: e.activation(
                out=sl(x1s, j), in_=pb[bank][:], func=AF.Identity,
                bias=bt["b_in"][:, j:j + 1], scale=1.0),
                waits=[("t", tmm)], inc=True)
            st_["x1_ready"][j] = ev
            bank_free[bank] = ev
            bank_free_ek[bank] = "s"

    def emit_gate_group(j, widx, dstb, bias_t):
        bank = next_bank()
        for k in range(KD):
            wts = [("s", st_["x1_ready"][k])]
            if k == 0 and bank_free[bank]:
                wts.append((bank_free_ek[bank], bank_free[bank]))
            op("t", lambda e, j=j, k=k, bank=bank, widx=widx: e.matmul(
                out=pb[bank][:], lhsT=watile(widx, k, j), rhs=sl(x1s, k),
                start=(k == 0), stop=(k == KD - 1)),
                waits=wts, inc=(k == KD - 1))
        tmm = counts["t"]
        ev = op("s", lambda e, j=j, dstb=dstb, bias_t=bias_t, bank=bank: e.activation(
            out=sl(dstb, j), in_=pb[bank][:], func=AF.Sigmoid,
            bias=bt[bias_t][:, j:j + 1], scale=1.0),
            waits=[("t", tmm)], inc=True)
        bank_free[bank] = ev
        bank_free_ek[bank] = "s"

    def emit_P2e_scan(b):
        for j in range(KD):
            op("s", lambda e, j=j: e.activation(
                out=sl(AS, j), in_=sl(GA, j), func=AF.Exp,
                bias=0.0, scale=cp[:, j:j + 1]))
        for j in range(KD):
            op("s", lambda e, j=j: e.activation(
                out=sl(GA, j), in_=sl(AS, j), func=AF.Sqrt, bias=1.0, scale=-1.0))
        for j in range(KD):
            op("s", lambda e, j=j: e.activation(
                out=sl(AS, j), in_=sl(AS, j), func=AF.Sqrt))
        sad = op("s", lambda e: e.drain(), inc=True)
        for j in range(KD):
            op("v", lambda e, j=j: e.tensor_mul(out=ut2[:], in0=sl(GX, j), in1=sl(x1s, j)),
               waits=[("s", sad)] if j == 0 else ())
            op("v", lambda e, j=j: e.tensor_mul(out=sl(BS, j), in0=ut2[:], in1=sl(GA, j)), inc=True)
        for j in range(KD):
            sc_v = op("v", lambda e, j=j: e.tensor_tensor_scan(
                out=sl(HS, j), data0=sl(AS, j), data1=sl(BS, j),
                initial=carry[:, j:j + 1], op0=OP.mult, op1=OP.add), inc=True)
            op("v", lambda e, j=j: e.tensor_copy(
                out=carry[:, j:j + 1], in_=HS[:, j * NT + NT - 1: j * NT + NT]),
                waits=[("v", sc_v)], inc=True)
        st_["vscan"] = counts["v"]

    def emit_LN1_x2(b):
        tbc1 = ln_finalize(st_["tstat1"], pstat_s1, pstat_q1, gw=st_["out_g"])
        for j in range(KD):
            op("v", lambda e, j=j: e.tensor_sub(out=sl(r16, j), in0=sl(r16, j), in1=M32[:]),
               waits=[("s", tbc1)] if j == 0 else ())
            op("v", lambda e, j=j: e.tensor_mul(out=ut1[:], in0=sl(r16, j), in1=S32[:]))
            if SIMQ:
                q_sim(sl(x2q8, j), ut1[:], 3, -6, inc=True)
            else:
                op("v", lambda e, j=j: e.tensor_copy(out=sl(x2q8, j), in_=ut1[:]), inc=True)
        st_["vx2"] = counts["v"]

    def emit_ffn1_group(b, hj):
        bank = next_bank()
        for k in range(KD):
            wts = [("v", st_["vx2"])] if k == 0 else []
            if k == 0 and bank_free[bank]:
                wts.append((bank_free_ek[bank], bank_free[bank]))
            op("t", lambda e, hj=hj, k=k, bank=bank: e.matmul(
                out=pb[bank][:], lhsT=w1tile(k, hj), rhs=sl(x2q8, k),
                start=(k == 0), stop=(k == KD - 1)),
                waits=wts, inc=(k == KD - 1))
        tmm = counts["t"]
        sev = op("s", lambda e, hj=hj, bank=bank: e.activation(
            out=t16a[:], in_=pb[bank][:], func=AF.Relu,
            bias=b1q[:, hj:hj + 1], scale=1.0),
            waits=[("t", tmm), ("v", last_vti[0])], inc=True)
        if SIMQ:
            last_vti[0] = q_sim(ut1[:], t16a[:], 10, -14, waits=[("s", sev)], inc=True)
            ev = q_sim(sl(hq8, hj), ut1[:], 3, -6, inc=True)
        else:
            ev = op("v", lambda e, hj=hj: e.tensor_copy(
                out=sl(hq8, hj), in_=t16a[:]), waits=[("s", sev)], inc=True)
            last_vti[0] = ev
        bank_free[bank] = ev
        bank_free_ek[bank] = "v"

    def emit_ffn2_unit(b, j):
        t0 = b * NT
        slot = len(mask_hist) % 2
        mwts = []
        if len(mask_hist) >= 2:
            mwts.append(("v", mask_hist[-2]))
        mt = dma("dy", "y", maskb[:, slot * NT:(slot + 1) * NT],
                 maskT[j * 128:(j + 1) * 128, t0:t0 + NT], waits=mwts)
        bank = next_bank()
        for hj in range(KH):
            wts = [("v", st_["vh"])] if hj == 0 else []
            if hj == 0 and bank_free[bank]:
                wts.append((bank_free_ek[bank], bank_free[bank]))
            op("t", lambda e, j=j, hj=hj, bank=bank: e.matmul(
                out=pb[bank][:], lhsT=w2tile(hj, j), rhs=sl(hq8, hj),
                start=(hj == 0), stop=(hj == KH - 1)),
                waits=wts, inc=(hj == KH - 1))
        tmm = counts["t"]
        sev = op("s", lambda e, j=j, bank=bank: e.activation(
            out=t16a[:], in_=pb[bank][:], func=AF.Identity,
            bias=b2q[:, j:j + 1], scale=1.0),
            waits=[("t", tmm), ("v", last_vti[0])], inc=True)
        if SIMQ:
            last_vti[0] = q_sim(ut1[:], t16a[:], 10, -14,
                                waits=[("s", sev), mt], inc=True)
        else:
            last_vti[0] = op("v", lambda e: e.tensor_copy(out=ut1[:], in_=t16a[:]),
                             waits=[("s", sev), mt], inc=True)
        mr = op("v", lambda e, slot=slot: e.tensor_mul(
            out=ut1[:], in0=ut1[:], in1=maskb[:, slot * NT:(slot + 1) * NT]), inc=True)
        mask_hist.append(mr)
        op("v", lambda e, j=j: e.tensor_mul(out=ut2[:], in0=sl(r16, j), in1=S32[:]))
        ev = op("v", lambda e, j=j: e.tensor_add(out=sl(r16, j), in0=ut1[:], in1=ut2[:]),
                inc=True)
        bank_free[bank] = ev
        bank_free_ek[bank] = "v"
        st_["ev2"][j] = ev
        if j > 1:
            ln_stats_tile(j - 2, lambda jj: sl(r16, jj), st_["ev2"][j - 2],
                          pstat_s2, pstat_q2)

    def emit_LN2_out(b):
        t0 = b * NT
        st_["tstat2"] = counts["t"]
        tbc2 = ln_finalize(st_["tstat2"], pstat_s2, pstat_q2)
        for j in range(KD):
            oslot = j % 2
            wts = [("s", tbc2)] if j == 0 else []
            if len(out_dma_t) >= 2:
                wts.append(out_dma_t[-2])
            osl = outst[:, oslot * NT:(oslot + 1) * NT]
            op("v", lambda e, j=j: e.tensor_sub(out=ut1[:], in0=sl(r16, j), in1=M32[:]),
               waits=wts)
            ev = op("v", lambda e, osl=osl: e.tensor_mul(out=osl, in0=ut1[:], in1=S32[:]),
                    inc=True)
            tl = dma("dg", "g", outT[j * 128:(j + 1) * 128, t0:t0 + NT],
                     osl, waits=[("v", ev)])
            out_dma_t.append(tl)

    for blk in range(NBLK):
        if blk == 0:
            emit_P1_loads(0)
            emit_P1_groups(0)
            for j in range(KD):
                emit_gate_group(j, 2, GA, "b_ga")
            for j in range(KD):
                emit_gate_group(j, 1, GX, "b_gx")
            emit_P2e_scan(0)
        emit_outproj(blk)
        if blk + 1 < NBLK:
            emit_P1_loads(blk + 1)
        emit_LN1_x2(blk)
        if blk + 1 < NBLK:
            emit_P1_groups(blk + 1)
        gate_units = ([(j, 2, GA, "b_ga") for j in range(KD)] +
                      [(j, 1, GX, "b_gx") for j in range(KD)]) if blk + 1 < NBLK else []
        gi = 0
        for hj in range(KH):
            emit_ffn1_group(blk, hj)
            if hj % 2 == 1 and gi < len(gate_units):
                emit_gate_group(*gate_units[gi])
                gi += 1
        while gi < len(gate_units):
            emit_gate_group(*gate_units[gi])
            gi += 1
        st_["vh"] = counts["v"]
        for j in range(KD):
            emit_ffn2_unit(blk, j)
            if j == 1 and blk + 1 < NBLK:
                emit_P2e_scan(blk + 1)
        ln_stats_tile(KD - 2, lambda jj: sl(r16, jj), st_["ev2"][KD - 2],
                      pstat_s2, pstat_q2)
        ln_stats_tile(KD - 1, lambda jj: sl(r16, jj), st_["ev2"][KD - 1],
                      pstat_s2, pstat_q2)
        emit_LN2_out(blk)

    finals = [(q + str(i), counts[q + str(i)]) for q in ("dy", "dg") for i in range(NDS)
              if counts[q + str(i)] > 0]

    def _final(eng, finals=tuple(finals)):
        for sk, val in finals:
            eng.wait_ge(SEM[sk], val)
    rec("g", _final)

    blk_ctx = stack.enter_context(nc.Block())

    def run_engine(ek):
        def section(eng):
            for k, fn in OPS:
                if k == ek:
                    fn(eng)
        return section

    blk_ctx.tensor(run_engine("t"))
    blk_ctx.vector(run_engine("v"))
    blk_ctx.scalar(run_engine("s"))
    blk_ctx.gpsimd(run_engine("g"))
    blk_ctx.sync(run_engine("y"))

    stack.close()
    return nc


def _make_mask():
    import jax
    cpu = jax.devices("cpu")[0]
    shape = (B, S, D)
    if os.environ.get("BASS_SIM_FP64") == "1":
        prev = jax.config.jax_enable_x64
        jax.config.update("jax_enable_x64", True)
        try:
            with jax.default_device(cpu):
                keep = np.asarray(jax.random.bernoulli(
                    jax.random.key(42, impl="threefry2x32"), 1.0 - 0.2, shape))
        finally:
            jax.config.update("jax_enable_x64", prev)
    else:
        with jax.default_device(cpu):
            keep = np.asarray(jax.random.bernoulli(
                jax.random.key(42, impl="threefry2x32"), 1.0 - 0.2, shape))
    return np.where(keep, np.float32(1.25), np.float32(0.0)).astype(ml_dtypes.float8_e4m3)


def kernel(L0_attn_x, L0_attn_W_in, L0_attn_b_in, L0_attn_W_gx, L0_attn_b_gx,
           L0_attn_W_ga, L0_attn_b_ga, L0_attn_a_param, L0_attn_W_out, L0_attn_b_out,
           L0_ffn_w1, L0_ffn_b1, L0_ffn_w2, L0_ffn_b2):
    x = np.asarray(L0_attn_x, np.float32)
    maskv = _make_mask()

    packKD = lambda v: np.ascontiguousarray(np.asarray(v, np.float32).reshape(KD, 128).T)
    shared = {
        "W_in": np.ascontiguousarray(np.asarray(L0_attn_W_in, np.float32)),
        "W_gx": np.ascontiguousarray(np.asarray(L0_attn_W_gx, np.float32)),
        "W_ga": np.ascontiguousarray(np.asarray(L0_attn_W_ga, np.float32)),
        "W_out": np.ascontiguousarray(np.asarray(L0_attn_W_out, np.float32)),
        "w1T": np.ascontiguousarray(np.asarray(L0_ffn_w1, np.float32).T),
        "w2T": np.ascontiguousarray(np.asarray(L0_ffn_w2, np.float32).T),
        "b_in_p": packKD(L0_attn_b_in),
        "b_gx_p": packKD(L0_attn_b_gx),
        "b_ga_p": packKD(L0_attn_b_ga),
        "b_out_p": packKD(L0_attn_b_out),
    }
    bias_pack = np.zeros((128, 512), np.float32)
    bias_pack[:, 0:8] = packKD(L0_ffn_b2)
    bias_pack[:, 8:40] = np.asarray(L0_ffn_b1, np.float32).reshape(32, 128).T
    bias_pack[:, 40:48] = packKD(L0_attn_a_param)
    shared["bias_pack"] = bias_pack
    in_maps = []
    for c in range(B):
        m = dict(shared)
        m["xT"] = np.ascontiguousarray(x[c].T)
        m["maskT"] = np.ascontiguousarray(maskv[c].T)
        in_maps.append(m)

    nc = build_program()
    res = run_bass_kernel_spmd(nc, in_maps, core_ids=list(range(B)))
    out = np.empty((B, S, D), np.float32)
    for c in range(B):
        out[c] = np.asarray(res.results[c]["outT"]).T
    return out


# version 14 (final)
# speedup vs baseline: 1.1511x; 1.0973x over previous
import os
import math
import numpy as np
import ml_dtypes
import concourse.bass as bass
import concourse.mybir as mybir
from concourse.bass_utils import run_bass_kernel_spmd

F32 = mybir.dt.float32
F16 = mybir.dt.float16
BF16 = mybir.dt.bfloat16
FP8 = mybir.dt.float8e4
I32 = mybir.dt.int32
AF = mybir.ActivationFunctionType
OP = mybir.AluOpType

SIMQ = os.environ.get("BASS_SIM_FP64") == "1"

B, S, D, H = 8, 4096, 1024, 4096
NT = 512
NBLK = S // NT
KD = D // 128
KH = H // 128
LN2C = 0.6931471805599453

ENG = ("t", "v", "s", "g", "y")


def build_program():
    nc = bass.Bass("TRN2", target_bir_lowering=False, debug=False, detect_race_conditions=False)

    xT = nc.dram_tensor("xT", [D, S], F32, kind="ExternalInput").ap()
    Wat = {}
    for nm in ("W_in", "W_gx", "W_ga", "W_out"):
        Wat[nm] = nc.dram_tensor(nm, [D, D], F32, kind="ExternalInput").ap()
    w1T = nc.dram_tensor("w1T", [D, H], F32, kind="ExternalInput").ap()
    w2T = nc.dram_tensor("w2T", [H, D], F32, kind="ExternalInput").ap()
    bp = {}
    for nm, w in (("b_in", KD), ("b_gx", KD), ("b_ga", KD), ("b_out", KD)):
        bp[nm] = nc.dram_tensor(nm + "_p", [128, w], F32, kind="ExternalInput").ap()
    biasp = nc.dram_tensor("bias_pack", [128, 512], F32, kind="ExternalInput").ap()
    maskT = nc.dram_tensor("maskT", [D, S], FP8, kind="ExternalInput").ap()
    outT = nc.dram_tensor("outT", [D, S], F32, kind="ExternalOutput").ap()

    import contextlib
    stack = contextlib.ExitStack()
    T = lambda name, shape, dt: stack.enter_context(nc.sbuf_tensor(name, shape, dt))

    wattn = T("wattn", [128, 4 * KD * D], F16)
    w1q = T("w1q", [128, KD * H], FP8)
    w2q = T("w2q", [128, KH * D], FP8)
    wstage = T("wstage", [128, 512], F32)
    bt = {nm: T("bt_" + nm, [128, KD], F32) for nm in ("b_in", "b_gx", "b_ga", "b_out")}
    b2q = T("b2q", [128, KD], F32)
    b1q = T("b1q", [128, KH], F32)
    cp = T("cp", [128, KD], F32)

    xs16 = T("xs16", [128, KD * NT], F16)
    AS = xs16
    x1s = T("x1s", [128, KD * NT], F16)
    GA = T("GA", [128, KD * NT], F16)
    HS = GA
    GX = T("GX", [128, KD * NT], F16)
    BS = GX
    r16 = T("r16", [128, KD * NT], F16)
    hq8 = T("hq8", [128, KH * NT], FP8)
    x2q8 = T("x2q8", [128, KD * NT], FP8)
    maskb = T("maskb", [128, 2 * NT], FP8)
    xstage = T("xstage", [128, NT], F32)
    outst = T("outst", [128, 2 * NT], F32)
    ti32 = T("ti32", [128, 512], I32) if SIMQ else None
    t16a = T("t16a", [128, NT], F16)
    xstage2 = T("xstage2", [128, NT], F32)
    ut1 = T("ut1", [128, NT], F32)
    ut2 = T("ut2", [128, NT], F32)
    tq16 = T("tq16", [128, NT], F16)
    carry = T("carry", [128, KD], F32)
    ones_hh = T("ones_hh", [128, 128], F16)
    M32 = wstage
    S32 = T("S32", [128, NT], F32)
    cst = T("cst", [128, 3], F32)

    P = lambda name: stack.enter_context(nc.psum_tensor(name, [128, NT], F32))
    pb = [P("pb0"), P("pb1"), P("pb2"), P("pb3")]
    pstat_s1 = P("pstat_s1")
    pstat_q1 = P("pstat_q1")
    pstat_s2 = P("pstat_s2")
    pstat_q2 = P("pstat_q2")

    SEM = {k: stack.enter_context(nc.semaphore("sem_" + k)) for k in ENG}
    NDS = 4
    for q in ("dy", "dg"):
        for i in range(NDS):
            SEM[q + str(i)] = stack.enter_context(nc.semaphore("sem_" + q + str(i)))

    counts = {k: 0 for k in SEM}
    dma_idx = {"dy": 0, "dg": 0}
    OPS = []

    def rec(ek, fn):
        OPS.append((ek, fn))

    def bump(ek):
        counts[ek] += 1
        return counts[ek]

    def op(ek, emit, waits=(), inc=False):
        w = tuple(waits)
        if inc:
            counts[ek] += 1

        def fn(eng, emit=emit, w=w, inc=inc):
            for sk, val in w:
                eng.wait_ge(SEM[sk], val)
            ins = emit(eng)
            if inc:
                ins.then_inc(SEM[ek], 1)
        rec(ek, fn)
        return counts[ek]

    def dma(qk, ek, out_ap, in_ap, waits=()):
        sk = qk + str(dma_idx[qk] % NDS)
        dma_idx[qk] += 1
        w = list(waits)
        if counts[sk] > 0:
            w.append((sk, counts[sk]))
        counts[sk] += 16
        tgt = counts[sk]
        w = tuple(w)

        def fn(eng, out_ap=out_ap, in_ap=in_ap, w=w, sk=sk):
            for k2, val in w:
                eng.wait_ge(SEM[k2], val)
            eng.dma_start(out=out_ap, in_=in_ap).then_inc(SEM[sk], 16)
        rec(ek, fn)
        return (sk, tgt)

    def barrier():
        vals = {}
        for k in ENG:
            counts[k] += 1
            vals[k] = counts[k]
        for k in ENG:
            def fn(eng, k=k, vals=dict(vals)):
                eng.sem_inc(SEM[k], 1)
                for k2, v in vals.items():
                    if k2 != k:
                        eng.wait_ge(SEM[k2], v)
            rec(k, fn)

    def sl(t_, j, w=NT):
        return t_[:, j * w:(j + 1) * w]

    def q_sim(dst_ap, src_ap, m_bits, min_e, waits=(), inc=False):
        phi = float(2.0 ** (52 - m_bits) + 1.0)
        Mg = float(1.5 * 2.0 ** (52 + min_e - m_bits))
        thr = float(2.0 ** min_e)
        A = ut2[:]
        Bb = ti32[:, 0:512]
        op("v", lambda e: e.tensor_scalar(out=A, in0=src_ap, scalar1=phi, scalar2=None, op0=OP.mult), waits=waits)
        op("v", lambda e: e.tensor_sub(out=Bb, in0=A, in1=src_ap))
        op("v", lambda e: e.tensor_sub(out=A, in0=A, in1=Bb))
        op("v", lambda e: e.tensor_scalar(out=Bb, in0=src_ap, scalar1=Mg, scalar2=None, op0=OP.add))
        op("v", lambda e: e.tensor_scalar(out=Bb, in0=Bb, scalar1=-Mg, scalar2=None, op0=OP.add))
        op("v", lambda e: e.tensor_scalar(out=dst_ap, in0=src_ap, scalar1=-1.0, scalar2=None, op0=OP.mult))
        op("v", lambda e: e.tensor_max(out=dst_ap, in0=dst_ap, in1=src_ap))
        op("v", lambda e: e.tensor_scalar(out=dst_ap, in0=dst_ap, scalar1=thr, scalar2=None, op0=OP.is_lt))
        op("v", lambda e: e.tensor_sub(out=Bb, in0=Bb, in1=A))
        op("v", lambda e: e.tensor_mul(out=Bb, in0=Bb, in1=dst_ap))
        return op("v", lambda e: e.tensor_add(out=dst_ap, in0=A, in1=Bb), inc=inc)

    watile = lambda widx, k, j: wattn[:, widx * KD * D + k * D + j * 128: widx * KD * D + k * D + j * 128 + 128]
    w1tile = lambda k, hj: w1q[:, k * H + hj * 128: k * H + hj * 128 + 128]
    w2tile = lambda hj, j: w2q[:, hj * D + j * 128: hj * D + j * 128 + 128]

    op("v", lambda e: e.memset(ones_hh[:], 1.0))
    op("v", lambda e: e.memset(carry[:], 0.0))
    op("v", lambda e: e.memset(cst[:, 0:1], 0.5))
    op("v", lambda e: e.memset(cst[:, 1:2], -0.5))
    op("v", lambda e: e.memset(cst[:, 2:3], 1e-5))

    for nm in ("b_in", "b_gx", "b_ga", "b_out"):
        dma("dy", "y", bt[nm][:], bp[nm])
    t_bp = dma("dy", "y", S32[:], biasp)

    op("s", lambda e: e.activation(out=cp[:], in_=S32[:, 40:48], func=AF.Exp), waits=[t_bp])
    cs = op("s", lambda e: e.activation(out=cp[:], in_=cp[:], func=AF.Ln, bias=1.0), inc=True)
    cs = op("s", lambda e: e.activation(out=cp[:], in_=cp[:], func=AF.Identity, scale=-16.0), inc=True)

    if SIMQ:
        q_sim(ut1[:], S32[:], 3, -6, waits=[t_bp])
    else:
        op("v", lambda e: e.tensor_copy(out=maskb[:, 0:512], in_=S32[:]), waits=[t_bp])
        op("v", lambda e: e.tensor_copy(out=ut1[:], in_=maskb[:, 0:512]))
    op("v", lambda e: e.tensor_copy(out=b2q[:], in_=ut1[:, 0:KD]))
    op("v", lambda e: e.tensor_copy(out=b1q[:], in_=ut1[:, KD:KD + KH]), inc=True)

    for widx, nm in enumerate(("W_in", "W_gx", "W_ga", "W_out")):
        for k in range(KD):
            for h in range(2):
                vprev = counts["v"]
                tl = dma("dg", "g", wstage[:],
                         Wat[nm][k * 128:(k + 1) * 128, h * 512:(h + 1) * 512],
                         waits=[("v", vprev)])
                off = widx * KD * D + k * D + h * 512
                op("v", lambda e, off=off: e.tensor_copy(
                    out=wattn[:, off:off + 512], in_=wstage[:]),
                    waits=[tl], inc=True)
    def qz8_big(dst_ap, tl):
        if SIMQ:
            return q_sim(dst_ap, wstage[:], 3, -6, waits=[tl], inc=True)
        return op("v", lambda e, dst_ap=dst_ap: e.tensor_copy(out=dst_ap, in_=wstage[:]), waits=[tl], inc=True)

    for k in range(KD):
        for r in range(H // 512):
            vprev = counts["v"]
            tl = dma("dg", "g", wstage[:], w1T[k * 128:(k + 1) * 128, r * 512:(r + 1) * 512],
                     waits=[("v", vprev)])
            qz8_big(w1q[:, k * H + r * 512: k * H + (r + 1) * 512], tl)
    for hk in range(KH):
        for r in range(2):
            vprev = counts["v"]
            tl = dma("dg", "g", wstage[:], w2T[hk * 128:(hk + 1) * 128, r * 512:(r + 1) * 512],
                     waits=[("v", vprev)])
            qz8_big(w2q[:, hk * D + r * 512: hk * D + (r + 1) * 512], tl)

    barrier()


    NB = 4
    bank_free = [0] * NB
    bank_free_ek = ["v"] * NB
    gbank = [0]

    def next_bank():
        bk = gbank[0] % NB
        gbank[0] += 1
        return bk

    last_stat_t = [0]
    last_vti = [0]
    out_dma_t = []
    mask_hist = []
    st_ = {"vscan": 0, "vx2": 0, "vh": 0, "tstat1": 0, "tstat2": 0,
           "xs_ready": [0] * KD, "x1_ready": [0] * KD,
           "ev1": [0] * KD, "ev2": [0] * KD, "out_g": 0,
           "xst": [None, None], "xsl": 0}

    def ln_stats_tile(j, src_fn, vready, ps, pq):
        swts = [("v", vready)]
        if last_stat_t[0]:
            swts.append(("t", last_stat_t[0]))
        c2 = op("s", lambda e, j=j: e.activation(
            out=tq16[:], in_=src_fn(j), func=AF.Square), waits=swts, inc=True)
        op("t", lambda e, j=j, ps=ps: e.matmul(
            out=ps[:], lhsT=ones_hh[:], rhs=src_fn(j),
            start=(j == 0), stop=(j == KD - 1)), waits=[("v", vready)])
        last_stat_t[0] = op("t", lambda e, j=j, pq=pq: e.matmul(
            out=pq[:], lhsT=ones_hh[:], rhs=tq16[:],
            start=(j == 0), stop=(j == KD - 1)), waits=[("s", c2)], inc=True)

    def ln_finalize(tstat_val, ps, pq, gw=0):
        fw = [("t", tstat_val)]
        if gw:
            fw.append(("g", gw))
        op("v", lambda e, ps=ps: e.tensor_scalar(
            out=M32[:], in0=ps[:], scalar1=1.0 / D, scalar2=None, op0=OP.mult),
            waits=fw)
        op("v", lambda e, pq=pq: e.tensor_scalar(
            out=S32[:], in0=pq[:], scalar1=1.0 / D, scalar2=None, op0=OP.mult))
        op("v", lambda e: e.tensor_mul(out=ut1[:], in0=M32[:], in1=M32[:]))
        op("v", lambda e: e.tensor_sub(out=ut1[:], in0=S32[:], in1=ut1[:]))
        vv = op("v", lambda e: e.drain(), inc=True)
        op("s", lambda e: e.activation(out=ut1[:], in_=ut1[:], func=AF.Ln, bias=cst[:, 2:3]),
           waits=[("v", vv)])
        op("s", lambda e: e.activation(out=S32[:], in_=ut1[:], func=AF.Exp, scale=cst[:, 1:2]))
        return op("s", lambda e: e.drain(), inc=True)

    def emit_outproj(b):
        t0 = b * NT
        for j in range(KD):
            bank = next_bank()
            xsl = st_["xsl"] % 2
            st_["xsl"] += 1
            xbuf = xstage if xsl == 0 else xstage2
            tl = dma("dy", "y", xbuf[:], xT[j * 128:(j + 1) * 128, t0:t0 + NT],
                     waits=[st_["xst"][xsl]] if st_["xst"][xsl] else [])
            for k in range(KD):
                wts = [("v", st_["vscan"])] if k == 0 else []
                if k == 0 and bank_free[bank]:
                    wts.append((bank_free_ek[bank], bank_free[bank]))
                op("t", lambda e, j=j, k=k, bank=bank: e.matmul(
                    out=pb[bank][:], lhsT=watile(3, k, j), rhs=sl(HS, k),
                    start=(k == 0), stop=(k == KD - 1)),
                    waits=wts, inc=(k == KD - 1))
            tmm = counts["t"]
            swts = [("t", tmm), tl]
            if j == 0 and st_["out_g"]:
                swts.append(("g", st_["out_g"]))
            ev = op("v", lambda e, j=j, bank=bank, xbuf=xbuf: e.scalar_tensor_tensor(
                out=sl(r16, j), in0=pb[bank][:], scalar=bt["b_out"][:, j:j + 1],
                in1=xbuf[:], op0=OP.add, op1=OP.add),
                waits=swts, inc=True)
            st_["xst"][xsl] = ("v", ev)
            bank_free[bank] = ev
            bank_free_ek[bank] = "v"
            st_["ev1"][j] = ev
            if j > 1:
                ln_stats_tile(j - 2, lambda jj: sl(r16, jj), st_["ev1"][j - 2],
                              pstat_s1, pstat_q1)
        ln_stats_tile(KD - 2, lambda jj: sl(r16, jj), st_["ev1"][KD - 2],
                      pstat_s1, pstat_q1)
        ln_stats_tile(KD - 1, lambda jj: sl(r16, jj), st_["ev1"][KD - 1],
                      pstat_s1, pstat_q1)
        st_["tstat1"] = counts["t"]

    def emit_P1_loads(b):
        t0 = b * NT
        for k in range(KD):
            xsl = st_["xsl"] % 2
            st_["xsl"] += 1
            xbuf = xstage if xsl == 0 else xstage2
            tl = dma("dy", "y", xbuf[:], xT[k * 128:(k + 1) * 128, t0:t0 + NT],
                     waits=[st_["xst"][xsl]] if st_["xst"][xsl] else [])
            c = op("v", lambda e, k=k, xbuf=xbuf: e.tensor_copy(
                out=sl(xs16, k), in_=xbuf[:]), waits=[tl], inc=True)
            st_["xs_ready"][k] = c
            st_["xst"][xsl] = ("v", c)

    def emit_P1_groups(b):
        for j in range(KD):
            bank = next_bank()
            for k in range(KD):
                wts = [("v", st_["xs_ready"][k])]
                if k == 0 and bank_free[bank]:
                    wts.append((bank_free_ek[bank], bank_free[bank]))
                op("t", lambda e, j=j, k=k, bank=bank: e.matmul(
                    out=pb[bank][:], lhsT=watile(0, k, j), rhs=sl(xs16, k),
                    start=(k == 0), stop=(k == KD - 1)),
                    waits=wts, inc=(k == KD - 1))
            tmm = counts["t"]
            ev = op("s", lambda e, j=j, bank=bank: e.activation(
                out=sl(x1s, j), in_=pb[bank][:], func=AF.Identity,
                bias=bt["b_in"][:, j:j + 1], scale=1.0),
                waits=[("t", tmm)], inc=True)
            st_["x1_ready"][j] = ev
            bank_free[bank] = ev
            bank_free_ek[bank] = "s"

    def emit_gate_group(j, widx, dstb, bias_t):
        bank = next_bank()
        for k in range(KD):
            wts = [("s", st_["x1_ready"][k])]
            if k == 0 and bank_free[bank]:
                wts.append((bank_free_ek[bank], bank_free[bank]))
            op("t", lambda e, j=j, k=k, bank=bank, widx=widx: e.matmul(
                out=pb[bank][:], lhsT=watile(widx, k, j), rhs=sl(x1s, k),
                start=(k == 0), stop=(k == KD - 1)),
                waits=wts, inc=(k == KD - 1))
        tmm = counts["t"]
        ev = op("s", lambda e, j=j, dstb=dstb, bias_t=bias_t, bank=bank: e.activation(
            out=sl(dstb, j), in_=pb[bank][:], func=AF.Sigmoid,
            bias=bt[bias_t][:, j:j + 1], scale=1.0),
            waits=[("t", tmm)], inc=True)
        bank_free[bank] = ev
        bank_free_ek[bank] = "s"

    def emit_scan_half(lo, hi):
        for j in range(lo, hi):
            sc_v = op("v", lambda e, j=j: e.tensor_tensor_scan(
                out=sl(HS, j), data0=sl(AS, j), data1=sl(BS, j),
                initial=carry[:, j:j + 1], op0=OP.mult, op1=OP.add), inc=True)
            op("v", lambda e, j=j: e.tensor_copy(
                out=carry[:, j:j + 1], in_=HS[:, j * NT + NT - 1: j * NT + NT]),
                waits=[("v", sc_v)], inc=True)
        st_["vscan"] = counts["v"]

    def emit_P2e_scan(b):
        for j in range(KD):
            op("s", lambda e, j=j: e.activation(
                out=sl(AS, j), in_=sl(GA, j), func=AF.Exp,
                bias=0.0, scale=cp[:, j:j + 1]))
        for j in range(KD):
            op("s", lambda e, j=j: e.activation(
                out=sl(GA, j), in_=sl(AS, j), func=AF.Sqrt, bias=1.0, scale=-1.0))
        for j in range(KD):
            op("s", lambda e, j=j: e.activation(
                out=sl(AS, j), in_=sl(AS, j), func=AF.Sqrt))
        sad = op("s", lambda e: e.drain(), inc=True)
        for j in range(KD):
            op("v", lambda e, j=j: e.tensor_mul(out=ut2[:], in0=sl(GX, j), in1=sl(x1s, j)),
               waits=[("s", sad)] if j == 0 else ())
            op("v", lambda e, j=j: e.tensor_mul(out=sl(BS, j), in0=ut2[:], in1=sl(GA, j)), inc=True)
        emit_scan_half(0, KD // 2)

    def emit_LN1_x2(b):
        tbc1 = ln_finalize(st_["tstat1"], pstat_s1, pstat_q1, gw=st_["out_g"])
        for j in range(KD):
            op("v", lambda e, j=j: e.tensor_sub(out=sl(r16, j), in0=sl(r16, j), in1=M32[:]),
               waits=[("s", tbc1)] if j == 0 else ())
            op("v", lambda e, j=j: e.tensor_mul(out=ut1[:], in0=sl(r16, j), in1=S32[:]))
            if SIMQ:
                q_sim(sl(x2q8, j), ut1[:], 3, -6, inc=True)
            else:
                op("v", lambda e, j=j: e.tensor_copy(out=sl(x2q8, j), in_=ut1[:]), inc=True)
        st_["vx2"] = counts["v"]

    def emit_ffn1_group(b, hj):
        bank = next_bank()
        for k in range(KD):
            wts = [("v", st_["vx2"])] if k == 0 else []
            if k == 0 and bank_free[bank]:
                wts.append((bank_free_ek[bank], bank_free[bank]))
            op("t", lambda e, hj=hj, k=k, bank=bank: e.matmul(
                out=pb[bank][:], lhsT=w1tile(k, hj), rhs=sl(x2q8, k),
                start=(k == 0), stop=(k == KD - 1)),
                waits=wts, inc=(k == KD - 1))
        tmm = counts["t"]
        sev = op("s", lambda e, hj=hj, bank=bank: e.activation(
            out=t16a[:], in_=pb[bank][:], func=AF.Relu,
            bias=b1q[:, hj:hj + 1], scale=1.0),
            waits=[("t", tmm), ("v", last_vti[0])], inc=True)
        if SIMQ:
            last_vti[0] = q_sim(ut1[:], t16a[:], 10, -14, waits=[("s", sev)], inc=True)
            ev = q_sim(sl(hq8, hj), ut1[:], 3, -6, inc=True)
        else:
            ev = op("v", lambda e, hj=hj: e.tensor_copy(
                out=sl(hq8, hj), in_=t16a[:]), waits=[("s", sev)], inc=True)
            last_vti[0] = ev
        bank_free[bank] = ev
        bank_free_ek[bank] = "v"

    def emit_ffn2_unit(b, j):
        t0 = b * NT
        slot = len(mask_hist) % 2
        mwts = []
        if len(mask_hist) >= 2:
            mwts.append(("v", mask_hist[-2]))
        mt = dma("dy", "y", maskb[:, slot * NT:(slot + 1) * NT],
                 maskT[j * 128:(j + 1) * 128, t0:t0 + NT], waits=mwts)
        bank = next_bank()
        for hj in range(KH):
            wts = [("v", st_["vh"])] if hj == 0 else []
            if hj == 0 and bank_free[bank]:
                wts.append((bank_free_ek[bank], bank_free[bank]))
            op("t", lambda e, j=j, hj=hj, bank=bank: e.matmul(
                out=pb[bank][:], lhsT=w2tile(hj, j), rhs=sl(hq8, hj),
                start=(hj == 0), stop=(hj == KH - 1)),
                waits=wts, inc=(hj == KH - 1))
        tmm = counts["t"]
        sev = op("s", lambda e, j=j, bank=bank: e.activation(
            out=t16a[:], in_=pb[bank][:], func=AF.Identity,
            bias=b2q[:, j:j + 1], scale=1.0),
            waits=[("t", tmm), ("v", last_vti[0])], inc=True)
        if SIMQ:
            last_vti[0] = q_sim(ut1[:], t16a[:], 10, -14,
                                waits=[("s", sev), mt], inc=True)
        else:
            last_vti[0] = op("v", lambda e: e.tensor_copy(out=ut1[:], in_=t16a[:]),
                             waits=[("s", sev), mt], inc=True)
        mr = op("v", lambda e, slot=slot: e.tensor_mul(
            out=ut1[:], in0=ut1[:], in1=maskb[:, slot * NT:(slot + 1) * NT]), inc=True)
        mask_hist.append(mr)
        op("v", lambda e, j=j: e.tensor_mul(out=ut2[:], in0=sl(r16, j), in1=S32[:]))
        ev = op("v", lambda e, j=j: e.tensor_add(out=sl(r16, j), in0=ut1[:], in1=ut2[:]),
                inc=True)
        bank_free[bank] = ev
        bank_free_ek[bank] = "v"
        st_["ev2"][j] = ev
        if j > 1:
            ln_stats_tile(j - 2, lambda jj: sl(r16, jj), st_["ev2"][j - 2],
                          pstat_s2, pstat_q2)

    def emit_LN2_out(b):
        t0 = b * NT
        st_["tstat2"] = counts["t"]
        tbc2 = ln_finalize(st_["tstat2"], pstat_s2, pstat_q2)
        for j in range(KD):
            oslot = j % 2
            wts = [("s", tbc2)] if j == 0 else []
            if len(out_dma_t) >= 2:
                wts.append(out_dma_t[-2])
            osl = outst[:, oslot * NT:(oslot + 1) * NT]
            op("v", lambda e, j=j: e.tensor_sub(out=ut1[:], in0=sl(r16, j), in1=M32[:]),
               waits=wts)
            ev = op("v", lambda e, osl=osl: e.tensor_mul(out=osl, in0=ut1[:], in1=S32[:]),
                    inc=True)
            tl = dma("dg", "g", outT[j * 128:(j + 1) * 128, t0:t0 + NT],
                     osl, waits=[("v", ev)])
            out_dma_t.append(tl)

    for blk in range(NBLK):
        if blk == 0:
            emit_P1_loads(0)
            emit_P1_groups(0)
            for j in range(KD):
                emit_gate_group(j, 2, GA, "b_ga")
            for j in range(KD):
                emit_gate_group(j, 1, GX, "b_gx")
            emit_P2e_scan(0)
            emit_scan_half(KD // 2, KD)
        emit_outproj(blk)
        if blk + 1 < NBLK:
            emit_P1_loads(blk + 1)
        emit_LN1_x2(blk)
        if blk + 1 < NBLK:
            emit_P1_groups(blk + 1)
        gate_units = ([(j, 2, GA, "b_ga") for j in range(KD)] +
                      [(j, 1, GX, "b_gx") for j in range(KD)]) if blk + 1 < NBLK else []
        gi = 0
        for hj in range(KH):
            emit_ffn1_group(blk, hj)
            if hj % 2 == 1 and gi < len(gate_units):
                emit_gate_group(*gate_units[gi])
                gi += 1
        while gi < len(gate_units):
            emit_gate_group(*gate_units[gi])
            gi += 1
        st_["vh"] = counts["v"]
        for j in range(KD):
            emit_ffn2_unit(blk, j)
            if j == 1 and blk + 1 < NBLK:
                emit_P2e_scan(blk + 1)
            if j == 4 and blk + 1 < NBLK:
                emit_scan_half(KD // 2, KD)
        ln_stats_tile(KD - 2, lambda jj: sl(r16, jj), st_["ev2"][KD - 2],
                      pstat_s2, pstat_q2)
        ln_stats_tile(KD - 1, lambda jj: sl(r16, jj), st_["ev2"][KD - 1],
                      pstat_s2, pstat_q2)
        emit_LN2_out(blk)

    finals = [(q + str(i), counts[q + str(i)]) for q in ("dy", "dg") for i in range(NDS)
              if counts[q + str(i)] > 0]

    def _final(eng, finals=tuple(finals)):
        for sk, val in finals:
            eng.wait_ge(SEM[sk], val)
    rec("g", _final)

    blk_ctx = stack.enter_context(nc.Block())

    def run_engine(ek):
        def section(eng):
            for k, fn in OPS:
                if k == ek:
                    fn(eng)
        return section

    blk_ctx.tensor(run_engine("t"))
    blk_ctx.vector(run_engine("v"))
    blk_ctx.scalar(run_engine("s"))
    blk_ctx.gpsimd(run_engine("g"))
    blk_ctx.sync(run_engine("y"))

    stack.close()
    return nc


def _make_mask():
    import jax
    cpu = jax.devices("cpu")[0]
    shape = (B, S, D)
    if os.environ.get("BASS_SIM_FP64") == "1":
        prev = jax.config.jax_enable_x64
        jax.config.update("jax_enable_x64", True)
        try:
            with jax.default_device(cpu):
                keep = np.asarray(jax.random.bernoulli(
                    jax.random.key(42, impl="threefry2x32"), 1.0 - 0.2, shape))
        finally:
            jax.config.update("jax_enable_x64", prev)
    else:
        with jax.default_device(cpu):
            keep = np.asarray(jax.random.bernoulli(
                jax.random.key(42, impl="threefry2x32"), 1.0 - 0.2, shape))
    return np.where(keep, np.float32(1.25), np.float32(0.0)).astype(ml_dtypes.float8_e4m3)


def kernel(L0_attn_x, L0_attn_W_in, L0_attn_b_in, L0_attn_W_gx, L0_attn_b_gx,
           L0_attn_W_ga, L0_attn_b_ga, L0_attn_a_param, L0_attn_W_out, L0_attn_b_out,
           L0_ffn_w1, L0_ffn_b1, L0_ffn_w2, L0_ffn_b2):
    x = np.asarray(L0_attn_x, np.float32)
    maskv = _make_mask()

    packKD = lambda v: np.ascontiguousarray(np.asarray(v, np.float32).reshape(KD, 128).T)
    shared = {
        "W_in": np.ascontiguousarray(np.asarray(L0_attn_W_in, np.float32)),
        "W_gx": np.ascontiguousarray(np.asarray(L0_attn_W_gx, np.float32)),
        "W_ga": np.ascontiguousarray(np.asarray(L0_attn_W_ga, np.float32)),
        "W_out": np.ascontiguousarray(np.asarray(L0_attn_W_out, np.float32)),
        "w1T": np.ascontiguousarray(np.asarray(L0_ffn_w1, np.float32).T),
        "w2T": np.ascontiguousarray(np.asarray(L0_ffn_w2, np.float32).T),
        "b_in_p": packKD(L0_attn_b_in),
        "b_gx_p": packKD(L0_attn_b_gx),
        "b_ga_p": packKD(L0_attn_b_ga),
        "b_out_p": packKD(L0_attn_b_out),
    }
    bias_pack = np.zeros((128, 512), np.float32)
    bias_pack[:, 0:8] = packKD(L0_ffn_b2)
    bias_pack[:, 8:40] = np.asarray(L0_ffn_b1, np.float32).reshape(32, 128).T
    bias_pack[:, 40:48] = packKD(L0_attn_a_param)
    shared["bias_pack"] = bias_pack
    in_maps = []
    for c in range(B):
        m = dict(shared)
        m["xT"] = np.ascontiguousarray(x[c].T)
        m["maskT"] = np.ascontiguousarray(maskv[c].T)
        in_maps.append(m)

    nc = build_program()
    res = run_bass_kernel_spmd(nc, in_maps, core_ids=list(range(B)))
    out = np.empty((B, S, D), np.float32)
    for c in range(B):
        out[c] = np.asarray(res.results[c]["outT"]).T
    return out
